# version 04
# baseline (speedup 1.0000x reference)
import contextlib

import numpy as np
import ml_dtypes

import concourse.bass as bass
import concourse.mybir as mybir
from concourse.bass_utils import run_bass_kernel_spmd

F32 = mybir.dt.float32
BF16 = mybir.dt.bfloat16
AF = mybir.ActivationFunctionType
ALU = mybir.AluOpType
AX = mybir.AxisListType

BFD = ml_dtypes.bfloat16

B, S, D, NH, DH = 4, 4096, 1024, 16, 64
NHP = NH // 2
TOK = 2048
NBLK = TOK // 128
GBLK = S // 128
NCH = D // 128
NCORES = 8
FRAME = 220
EPS_LN = 1e-5
EPS_RMS = 1e-6
CAP = float(B * S // 8)
ENGS = ("te", "ve", "se", "gp", "sy")
N_LAYERS = 2



def _perm_kq():
    perm = []
    for hp in range(NHP):
        for h in (2 * hp, 2 * hp + 1):
            perm += [h * DH + 2 * i for i in range(32)]
            perm += [h * DH + 2 * i + 1 for i in range(32)]
    return np.array(perm, dtype=np.int64)


def _own_positions(parity):
    b = np.arange(NBLK) * 2 + parity
    return (b[:, None] * 128 + np.arange(128)[None, :]).reshape(-1)


def _fid(layer):
    pos = np.arange(S)
    return pos if layer == 0 else pos // FRAME


def _ext_tables(layer):
    fid = _fid(layer)
    ext = []
    for i in range(NBLK):
        e = 0
        for p in (0, 1):
            qmax = 128 * (2 * i + p) + 127
            kmax = qmax if layer == 0 else min(
                S - 1, (int(fid[qmax]) + 1) * FRAME - 1)
            e = max(e, kmax // 128 + 1)
        ext.append(int(e))
    return ext


def _mask_plan(layer):
    fid = _fid(layer)
    ext = _ext_tables(layer)
    raw = {}
    for p in (0, 1):
        for i in range(NBLK):
            fq = fid[128 * (2 * i + p):128 * (2 * i + p) + 128]
            for k in range(ext[i]):
                fk = fid[128 * k:128 * k + 128]
                m = (fk[:, None] <= fq[None, :])
                if not m.all():
                    raw.setdefault((k, i), {})[p] = m
    ones = np.ones((128, 128), bool)
    uniq, slots = {}, {}
    for key in sorted(raw):
        d = raw[key]
        sig = (d.get(0, ones).tobytes(), d.get(1, ones).tobytes())
        if sig not in uniq:
            uniq[sig] = len(uniq)
        slots[key] = uniq[sig]
    return slots, len(uniq)


_MASK_SLOTS = {L: _mask_plan(L) for L in range(N_LAYERS)}


def _mask_consts(layer, parity):
    fid = _fid(layer)
    slots, nu = _MASK_SLOTS[layer]
    out = np.ones((128, 128 * max(nu, 1)), dtype=np.float32)
    done = set()
    for (k, i), u in slots.items():
        if u in done:
            continue
        done.add(u)
        fq = fid[128 * (2 * i + parity):128 * (2 * i + parity) + 128]
        fk = fid[128 * k:128 * k + 128]
        out[:, 128 * u:128 * u + 128] = (
            fk[:, None] <= fq[None, :]).astype(np.float32)
    return out


def _sub_lo(layer, qc, k):
    ext = _ext_tables(layer)
    for j in range(8):
        if ext[8 * qc + j] > k:
            return 128 * j
    return 1024



class Sem:
    __slots__ = ("h", "n")

    def __init__(self, nc, ctx, name):
        self.h = ctx.enter_context(nc.semaphore(name))
        self.n = 0


class Bump:
    def __init__(self, tile, n):
        self.tile, self.cap, self.off = tile, n, 0

    def take(self, k):
        assert self.off + k <= self.cap, ("arena overflow", self.off, k)
        ap = self.tile[:, self.off:self.off + k]
        self.off += k
        return ap

    def reset(self, off=0):
        self.off = off


def _pieces(lo, hi, step=512):
    out = []
    while lo < hi:
        nxt = min(hi, (lo // step + 1) * step)
        out.append((lo, nxt))
        lo = nxt
    return out


class MK:
    def __init__(self, taps=(), n_layers=N_LAYERS, tap_layer=0, dump=()):
        nc = bass.Bass("TRN2", target_bir_lowering=False,
                       detect_race_conditions=False)
        self.nc = nc
        self.ctx = contextlib.ExitStack()
        self.taps = set(taps)
        self.n_layers = n_layers
        self.tap_layer = tap_layer
        self.dump = set(dump)
        self.q = {e: [] for e in ENGS}
        self.sem = {}
        for name in ("dwi", "dxi", "dxo", "dxt", "w0", "w1", "w2", "w3",
                     "kb", "rb0", "rb1", "ac", "la", "ct", "ge", "hs",
                     "hr0", "hr1", "te", "ve", "se", "cc"):
            self.sem[name] = Sem(nc, self.ctx, name)
        self._declare_io()
        self._declare_sbuf()

    def emit(self, eng, fn):
        self.q[eng].append(fn)

    def wait(self, eng, sname, val):
        if val and val > 0:
            s = self.sem[sname]
            self.q[eng].append(lambda e, s=s, v=val: e.wait_ge(s.h, v))

    def inc(self, eng, fn, sname, k=1):
        s = self.sem[sname]
        self.q[eng].append(lambda e, fn=fn, s=s, k=k: fn(e).then_inc(s.h, k))
        s.n += k
        return s.n

    def n(self, sname):
        return self.sem[sname].n

    def vnop(self):
        self.q["ve"].append(lambda e: e.engine_nop())

    def _declare_io(self):
        nc = self.nc

        def inp(name, shape, dt=F32):
            return nc.dram_tensor(name, list(shape), dt,
                                  kind="ExternalInput").ap()

        self.x_in = inp("x", [NCH, 128, TOK])
        self.csB_in = inp("csB", [128, TOK])
        self.snB_in = inp("snB", [128, TOK])
        self.constf_in = inp("constf", [128, 532])
        self.h16_in = inp("h16", [128, 768])
        self.nmask = {L: _MASK_SLOTS[L][1] for L in range(N_LAYERS)}
        self.maskc_in = {
            L: inp("maskc%d" % L, [128, 128 * max(1, self.nmask[L])])
            for L in range(self.n_layers)}
        self.maskx_in = inp("maskx", [128, 128])
        self.lcol_in = inp("lcol", [128, 16])
        self.sew_in = inp("sew", [1, 2 * TOK])

        w = {}
        w["l0_kq"] = inp("l0_kq", [16, 128, D])
        w["l0_v"] = inp("l0_v", [NCH, 128, D])
        w["l0_o"] = inp("l0_o", [NCH, 128, D])
        w["l0_wg"] = inp("l0_wg", [1, 128, 64])
        w["l0_b"] = inp("l0_b", [8, D])
        w["l0_w"] = inp("l0_w", [8, 128, 8 * D])
        if self.n_layers > 1:
            w["l1_kq"] = inp("l1_kq", [16, 128, D])
            w["l1_v"] = inp("l1_v", [NCH, 128, D])
            w["l1_o"] = inp("l1_o", [NCH, 128, D])
            w["l1_w1"] = inp("l1_w1", [16, 128, 4096])
            w["l1_w2"] = inp("l1_w2", [2, 8, 128, 4096])
        self.w = w

        self.out_x = nc.dram_tensor("out_x", [NCH, 128, TOK], F32,
                                    kind="ExternalOutput").ap()
        self.tap_out = {}
        for t in self.taps:
            shp = {"xn": [NCH, 128, TOK], "kq": [16, 128, TOK],
                   "v": [NBLK, 128, D], "acts": [NHP, 128, TOK],
                   "route": [128, 128], "scr0": [128, 2048], "x1": [NCH, 128, TOK],
                   "x2": [NCH, 128, TOK], "x3": [NCH, 128, TOK]}[t]
            dt = F32 if t in ("route", "scr0", "x1", "x2", "x3") else BF16
            self.tap_out[t] = nc.dram_tensor("tap_" + t, shp, dt,
                                             kind="ExternalOutput").ap()

        def idram(name, shape, dt):
            kind = "ExternalOutput" if name in self.dump else "Internal"
            return nc.dram_tensor(name, shape, dt, kind=kind)

        self.kvin = idram("kvin", [NHP, 128, 4096], BF16)
        self.kvout = idram("kvout", [NHP, 256, 4096], BF16)
        self.qspill = idram("qspill", [NHP, 128, TOK], BF16)
        self.actsb = idram("actsb", [NHP, 128, TOK], BF16)
        self.cntin = idram("cntin", [1, 128], F32)
        self.cntout = idram("cntout", [8, 128], F32)
        self.gbounce = idram("gbounce", [8, TOK], BF16)
        self.hspill = idram("hspill", [8, 128, 8 * TOK], BF16)

    def _declare_sbuf(self):
        nc = self.nc
        sb = nc.alloc_sbuf_tensor
        self.x_t = sb("x_t", [128, NCH * TOK], F32)[:, :]
        self.csB = sb("csB_t", [128, TOK], BF16)[:, :]
        self.snB = sb("snB_t", [128, TOK], BF16)[:, :]
        self.constf = sb("constf_t", [128, 532], F32)[:, :]
        self.h16 = sb("h16_t", [128, 768], BF16)[:, :]
        self.maskc = {
            L: sb("mk%d" % L, [128, 128 * max(1, self.nmask[L])], BF16)[:, :]
            for L in range(self.n_layers)}
        self.maskx = sb("maskx_t", [128, 128], F32)[:, :]
        self.lcol = sb("lcol_t", [128, 16], F32)[:, :]
        self.arn_t = sb("arn", [128, 36880], BF16)[:, :]
        self.scr_t = sb("scr", [128, 2048], F32)[:, :]
        self.wb_t = sb("wbuf", [128, 8448], BF16)[:, :]
        self.rows = sb("rows", [128, 4096], F32)[:, :]
        self.ps8 = nc.alloc_psum_tensor("ps8", [128, 4096], F32)[:, :]

        c = self.constf
        self.onesF = c[:, 0:128]
        self.ltri = c[:, 128:256]
        self.identF = c[:, 256:384]
        self.wvec = c[:, 384:392]
        self.evec = c[:, 392:400]
        self.hselF = c[:, 400:528]
        self.epsln = c[:, 528:529]
        self.epsrms = c[:, 529:530]
        self.onesB = self.h16[:, 0:1]
        self.onesRowB = self.h16[0:1, 0:128]
        self.hselB = self.h16[:, 128:130]
        self.arn = Bump(self.arn_t, 36880)
        self.scr = Bump(self.scr_t, 2048)
        self.wb = Bump(self.wb_t, 8448)

    def wload(self, src_ap, ncols, sname):
        dst = self.wb.take(ncols)
        tok = self.inc("gp", lambda e, d=dst, s=src_ap: e.dma_start(
            out=d, in_=s), sname, 16)
        return dst, tok

    def te_tok(self):
        s = self.sem["te"]
        fn = self.q["te"].pop()
        self.q["te"].append(
            lambda e, fn=fn, s=s: fn(e).then_inc(s.h, 1))
        s.n += 1
        return s.n

    def init_loads(self):
        p = self
        for c in range(NCH):
            p.inc("sy", lambda e, c=c: e.dma_start(
                out=p.x_t[:, c * TOK:(c + 1) * TOK], in_=p.x_in[c]),
                "dxi", 16)
        p.inc("gp", lambda e: e.dma_start(out=p.csB[:], in_=p.csB_in),
              "dwi", 16)
        p.inc("gp", lambda e: e.dma_start(out=p.snB[:], in_=p.snB_in),
              "dwi", 16)
        p.inc("sy", lambda e: e.dma_start(out=p.constf[:], in_=p.constf_in),
              "dxi", 16)
        p.inc("gp", lambda e: e.dma_start(out=p.h16[:], in_=p.h16_in),
              "dwi", 16)
        for L in range(p.n_layers):
            p.inc("gp", lambda e, L=L: e.dma_start(
                out=p.maskc[L][:], in_=p.maskc_in[L]), "dwi", 16)
        p.inc("sy", lambda e: e.dma_start(out=p.maskx[:], in_=p.maskx_in),
              "dxi", 16)
        p.inc("sy", lambda e: e.dma_start(out=p.lcol[:], in_=p.lcol_in),
              "dxi", 16)
        self.sewrow = p.rows[32:33, 0:2048]
        sd = p.inc("sy", lambda e: e.dma_start(
            out=p.rows[32:33, 0:4096], in_=p.sew_in), "dxi", 16)
        for eng in ("te", "ve", "se"):
            p.wait(eng, "dxi", p.n("dxi"))
            p.wait(eng, "dwi", p.n("dwi"))
        p.inc("ve", lambda e: e.tensor_tensor(
            self.sewrow, self.sewrow, p.rows[32:33, 2048:4096],
            op=ALU.add), "ve")

    def ln_stats(self):
        p = self
        mrow = p.rows[0:1, 0:2048]
        srow = p.rows[0:1, 2048:4096]
        xb = p.arn_t[:, 36880 - TOK:36880]
        sq = p.arn_t[:, 36880 - 2 * TOK:36880 - TOK]
        p.wait("te", "ve", p.n("ve"))
        prev_te = 0
        for c in range(NCH):
            p.wait("ve", "te", prev_te)
            tv = p.inc("ve", lambda e, c=c: e.tensor_copy(
                xb, p.x_t[:, c * TOK:(c + 1) * TOK]), "ve")
            p.wait("se", "ve", tv)
            p.wait("se", "te", prev_te)
            sv = p.inc("se", lambda e: e.activation(sq, xb, AF.Square), "se")
            p.wait("te", "se", sv)
            for (lo, hi) in _pieces(0, TOK):
                p.emit("te", lambda e, c=c, lo=lo, hi=hi: e.matmul(
                    p.ps8[0:1, lo:hi], p.onesB, xb[:, lo:hi],
                    start=(c == 0), stop=(c == NCH - 1)))
                p.emit("te", lambda e, c=c, lo=lo, hi=hi: e.matmul(
                    p.ps8[0:1, 2048 + lo:2048 + hi], p.onesB, sq[:, lo:hi],
                    start=(c == 0), stop=(c == NCH - 1)))
            prev_te = p.te_tok()
        tt = prev_te
        p.wait("ve", "te", tt)
        p.emit("ve", lambda e: e.tensor_scalar(
            mrow, p.ps8[0:1, 0:2048], 1.0 / D, None, op0=ALU.mult))
        p.emit("ve", lambda e: e.tensor_scalar(
            srow, p.ps8[0:1, 2048:4096], 1.0 / D, None, op0=ALU.mult))
        p.emit("ve", lambda e: e.tensor_tensor(
            p.ps8[0:1, 0:512], mrow[:, 0:512], mrow[:, 0:512], op=ALU.mult))
        p.emit("ve", lambda e: e.tensor_tensor(
            p.ps8[0:1, 512:1024], mrow[:, 512:1024], mrow[:, 512:1024],
            op=ALU.mult))
        p.emit("ve", lambda e: e.tensor_tensor(
            p.ps8[0:1, 1024:1536], mrow[:, 1024:1536], mrow[:, 1024:1536],
            op=ALU.mult))
        p.emit("ve", lambda e: e.tensor_tensor(
            p.ps8[0:1, 1536:2048], mrow[:, 1536:2048], mrow[:, 1536:2048],
            op=ALU.mult))
        p.vnop()
        vv = p.inc("ve", lambda e: e.tensor_tensor(
            srow, srow, p.ps8[0:1, 0:2048], op=ALU.subtract), "ve")
        p.wait("se", "ve", vv)
        p.emit("se", lambda e: e.activation(srow, srow, AF.Ln, bias=p.epsln[0:1, :]))
        sv = p.inc("se", lambda e: e.activation(
            srow, srow, AF.Exp, scale=-0.5), "se")
        return sv

    def bcast_ms_psum(self, stats_se, lo, hi):
        p = self
        n = hi - lo
        assert n <= 512
        p.wait("te", "se", stats_se)
        p.emit("te", lambda e: e.matmul(
            p.ps8[:, 3072:3072 + n], p.onesF[0:1, 0:128],
            p.rows[0:1, lo:hi], start=True, stop=True))
        p.emit("te", lambda e: e.matmul(
            p.ps8[:, 3584:3584 + n], p.onesF[0:1, 0:128],
            p.rows[0:1, 2048 + lo:2048 + hi], start=True, stop=True))
        return self.te_tok()

    def xn_full(self, stats_se, xn_off):
        p = self
        vv = 0
        t1p = [p.arn_t[:, 36880 - TOK:36880 - TOK + 512],
               p.arn_t[:, 36880 - TOK + 512:36880 - TOK + 1024]]
        for (lo, hi) in _pieces(0, TOK):
            tt = self.bcast_ms_psum(stats_se, lo, hi)
            p.wait("ve", "te", tt)
            n = hi - lo
            for c0 in range(0, NCH, 2):
                for ci in (c0, c0 + 1):
                    xc = p.x_t[:, ci * TOK + lo:ci * TOK + hi]
                    t1 = t1p[ci % 2]
                    p.emit("ve", lambda e, xc=xc, n=n, t1=t1: e.tensor_tensor(
                        t1[:, 0:n], xc, p.ps8[:, 3072:3072 + n],
                        op=ALU.subtract))
                for ci in (c0, c0 + 1):
                    xnc = p.arn_t[:, xn_off + ci * TOK + lo:
                                  xn_off + ci * TOK + hi]
                    t1 = t1p[ci % 2]
                    vv = p.inc("ve", lambda e, xnc=xnc, n=n, t1=t1:
                               e.tensor_tensor(
                        xnc, t1[:, 0:n], p.ps8[:, 3584:3584 + n],
                        op=ALU.mult), "ve")
            p.wait("te", "ve", vv)
        return vv

    def attention(self, L):
        p = self
        slots, _ = _MASK_SLOTS[L]
        ext = _ext_tables(L)
        kqw = p.w["l%d_kq" % L]
        vw = p.w["l%d_v" % L]
        ow = p.w["l%d_o" % L]

        p.arn.reset(0)
        p.scr.reset(0)
        p.wb.reset(0)
        stats_se = p.ln_stats()
        xn_off = 0
        p.arn.take(NCH * TOK)
        vv_xn = p.xn_full(stats_se, xn_off)

        if "xn" in p.tap_out and L == p.tap_layer:
            p.wait("sy", "ve", vv_xn)
            for c in range(NCH):
                p.inc("sy", lambda e, c=c: e.dma_start(
                    out=p.tap_out["xn"][c],
                    in_=p.arn_t[:, xn_off + c * TOK:xn_off + (c + 1) * TOK]),
                    "dxt", 16)

        if L == 0:
            csK = csQ = p.csB
            snK = snQ = p.snB
            tab_ve = 0
        else:
            csQ = p.arn.take(TOK)
            snQ = p.arn.take(TOK)
            csK = p.arn.take(TOK)
            snK = p.arn.take(TOK)
            p.emit("ve", lambda e: e.tensor_scalar(
                csQ, p.csB, p.lcol[:, 0:1], None, op0=ALU.mult))
            p.emit("ve", lambda e: e.tensor_scalar(
                snQ, p.snB, p.lcol[:, 1:2], None, op0=ALU.mult))
            p.emit("ve", lambda e: e.tensor_scalar(
                csK, p.csB, p.lcol[:, 2:3], None, op0=ALU.mult))
            tab_ve = p.inc("ve", lambda e: e.tensor_scalar(
                snK, p.snB, p.lcol[:, 3:4], None, op0=ALU.mult), "ve")

        rotTa = p.arn.take(TOK)
        rotTs = p.arn.take(TOK)
        rotU1 = p.arn.take(TOK)
        rotU2 = p.arn.take(TOK)
        ytile = p.arn.take(TOK)
        sqt = p.arn.take(TOK)
        mxK = p.rows[0:1, 4094:4095]
        mxQ = p.rows[0:1, 4095:4096]
        iv = p.inc("ve", lambda e: e.memzero(p.rows[0:1, 4092:4096]), "ve")

        buf_free_ve = [max(iv, tab_ve), max(iv, tab_ve)]
        k_dma = []
        q_dma = []

        mm_toks = {}
        last_csq = 0
        kq_entry_te = p.n("te")
        for oc in range(16):
            p.wb.reset((oc % 2) * D)
            p.wait("gp", "te", mm_toks.get(oc - 2, kq_entry_te))
            pan, wtok = p.wload(kqw[oc], D, "w%d" % (oc % 2))
            buf = oc % 2
            pbase = 2048 * buf
            isK = oc < 8
            p.wait("te", "w%d" % (oc % 2), wtok)
            p.wait("te", "ve", vv_xn)
            p.wait("te", "ve", buf_free_ve[buf])
            for c in range(NCH):
                for (lo, hi) in _pieces(0, TOK):
                    p.emit("te", lambda e, c=c, lo=lo, hi=hi, pan=pan,
                           pbase=pbase: e.matmul(
                        p.ps8[:, pbase + lo:pbase + hi],
                        pan[:, c * 128:(c + 1) * 128],
                        p.arn_t[:, xn_off + c * TOK + lo:
                                xn_off + c * TOK + hi],
                        start=(c == 0), stop=(c == NCH - 1)))
            mm_tok = p.te_tok()
            mm_toks[oc] = mm_tok

            p.wait("ve", "te", mm_tok)
            p.wait("ve", "se", last_csq)
            ta_ve = p.inc("ve", lambda e, pbase=pbase: e.tensor_copy(
                rotTa, p.ps8[:, pbase:pbase + TOK]), "ve")
            tsh_ve = ta_ve
            for g in range(4):
                a, b_ = 32 * g, 32 * (g ^ 1)
                tsh_ve = p.inc("ve", lambda e, a=a, b_=b_, pbase=pbase:
                               e.tensor_copy(
                    rotTs[a:a + 32, :],
                    p.ps8[b_:b_ + 32, pbase:pbase + TOK]), "ve")
            cs = csK if isK else csQ
            sn = snK if isK else snQ
            p.emit("ve", lambda e, cs=cs: e.tensor_tensor(
                rotU1, rotTa, cs, op=ALU.mult))
            p.emit("ve", lambda e, sn=sn: e.tensor_tensor(
                rotU2, rotTs, sn, op=ALU.mult))
            p.vnop()
            y_ve = p.inc("ve", lambda e: e.tensor_tensor(
                ytile, rotU1, rotU2, op=ALU.subtract), "ve")
            buf_free_ve[buf] = tsh_ve

            if L == 1:
                p.wait("se", "ve", y_ve)
                sq_se = p.inc("se", lambda e: e.activation(
                    sqt, rotTa, AF.Square), "se")
                obase = 2048 * (1 - buf)
                p.wait("te", "se", sq_se)
                p.wait("te", "ve", buf_free_ve[1 - buf])
                for (lo, hi) in _pieces(0, TOK):
                    p.emit("te", lambda e, lo=lo, hi=hi, obase=obase:
                           e.matmul(p.ps8[0:2, obase + lo:obase + hi],
                                    p.hselB, sqt[:, lo:hi],
                                    start=True, stop=True))
                nm_tok = p.te_tok()
                p.wait("se", "te", nm_tok)
                rsrow = p.rows[0:2, 0:2048]
                p.emit("se", lambda e, obase=obase, rsrow=rsrow: e.activation(
                    rsrow, p.ps8[0:2, obase:obase + TOK], AF.Ln,
                    bias=p.epsrms[0:2, :], scale=1.0 / DH))
                rs_se = p.inc("se", lambda e, rsrow=rsrow: e.activation(
                    rsrow, rsrow, AF.Exp, scale=-0.5), "se")
                p.wait("te", "se", rs_se)
                for (lo, hi) in _pieces(0, TOK):
                    p.emit("te", lambda e, lo=lo, hi=hi, obase=obase,
                           rsrow=rsrow: e.matmul(
                        p.ps8[:, obase + lo:obase + hi],
                        p.hselF[0:2, 0:128],
                        rsrow[:, lo:hi], start=True, stop=True))
                rb_tok = p.te_tok()
                p.wait("ve", "te", rb_tok)
                y_ve = p.inc("ve", lambda e, obase=obase: e.tensor_tensor(
                    ytile, ytile, p.ps8[:, obase:obase + TOK], op=ALU.mult),
                    "ve")
                buf_free_ve[1 - buf] = y_ve

            p.wait("se", "ve", y_ve)
            csq_se = p.inc("se", lambda e: e.activation(
                sqt, ytile, AF.Square), "se")
            cbase = pbase
            p.wait("te", "se", csq_se)
            for (lo, hi) in _pieces(0, TOK):
                p.emit("te", lambda e, lo=lo, hi=hi, cbase=cbase: e.matmul(
                    p.ps8[0:1, cbase + lo:cbase + hi], p.onesB,
                    sqt[:, lo:hi], start=True, stop=True))
            cn_tok = p.te_tok()
            last_csq = csq_se
            p.wait("ve", "te", cn_tok)
            mx = mxK if isK else mxQ
            p.emit("ve", lambda e, cbase=cbase: e.reduce_max(
                p.rows[0:1, 4092:4093], p.ps8[0:1, cbase:cbase + TOK],
                axis=AX.X))
            p.vnop()
            cm_ve = p.inc("ve", lambda e, mx=mx: e.tensor_max(
                mx, mx, p.rows[0:1, 4092:4093]), "ve")
            buf_free_ve[buf] = cm_ve

            p.wait("sy", "ve", y_ve)
            if isK:
                k_dma.append(p.inc("sy", lambda e, oc=oc: e.dma_start(
                    out=p.kvin.ap()[oc][:, 0:2048], in_=ytile), "kb", 16))
            else:
                q_dma.append(p.inc("sy", lambda e, oc=oc: e.dma_start(
                    out=p.qspill.ap()[oc - 8], in_=ytile), "kb", 16))
            if "kq" in p.tap_out and L == p.tap_layer:
                p.inc("sy", lambda e, oc=oc: e.dma_start(
                    out=p.tap_out["kq"][oc], in_=ytile), "dxt", 16)
            p.wait("ve", "kb", p.n("kb"))

        p.vnop()
        cv = p.inc("ve", lambda e: e.tensor_tensor(
            p.rows[0:1, 4093:4094], mxK, mxQ, op=ALU.mult), "ve")
        p.wait("se", "ve", cv)
        p.emit("se", lambda e: e.activation(
            p.rows[0:1, 4093:4094], p.rows[0:1, 4093:4094], AF.Ln))
        cb_se = p.inc("se", lambda e: e.activation(
            p.rows[0:1, 4093:4094], p.rows[0:1, 4093:4094], AF.Exp,
            scale=0.5), "se")
        p.wait("ve", "se", cb_se)
        nv = p.inc("ve", lambda e: e.tensor_scalar(
            p.rows[0:1, 4093:4094], p.rows[0:1, 4093:4094], -0.125, None,
            op0=ALU.mult), "ve")
        p.wait("te", "ve", nv)
        p.emit("te", lambda e: e.matmul(
            p.ps8[:, 3072:3073], p.onesF[0:1, 0:128],
            p.rows[0:1, 4093:4094], start=True, stop=True))
        cb_te = p.te_tok()
        ccol = p.rows[:, 4091:4092]
        p.wait("ve", "te", cb_te)
        ccol_ve = p.inc("ve", lambda e, ccol=ccol: e.tensor_copy(
            ccol, p.ps8[:, 3072:3073]), "ve")

        p.wb.reset(0)
        p.wait("gp", "te", mm_toks[15])
        buf_free_ve[1] = max(buf_free_ve[1], ccol_ve)
        vimg, vtoks = [], []
        for c in range(NCH):
            im, tk = p.wload(vw[c], D, "w2")
            vimg.append(im)
            vtoks.append(tk)
        v_dma = []
        vstage = [rotU1, rotU2]
        last_vve = [p.n("ve"), p.n("ve")]
        for tb in range(NBLK):
            sbuf_slot = tb % 2
            vbase = 2048 * (tb % 2)
            p.wait("te", "w2", vtoks[-1])
            p.wait("te", "ve", max(buf_free_ve[tb % 2], last_vve[sbuf_slot]))
            for c in range(NCH):
                for half in range(2):
                    p.emit("te", lambda e, c=c, half=half, tb=tb,
                           vbase=vbase: e.matmul(
                        p.ps8[:, vbase + half * 512:vbase + half * 512 + 512],
                        p.arn_t[:, xn_off + c * TOK + tb * 128:
                                xn_off + c * TOK + tb * 128 + 128],
                        vimg[c][:, half * 512:half * 512 + 512],
                        start=(c == 0), stop=(c == NCH - 1)))
            vm_tok = p.te_tok()
            buf_free_ve[tb % 2] = 0
            p.wait("ve", "te", vm_tok)
            p.wait("ve", "kb", p.n("kb"))
            vst = vstage[sbuf_slot][:, 0:1024]
            ve_tok = p.inc("ve", lambda e, vst=vst, vbase=vbase:
                           e.tensor_copy(vst, p.ps8[:, vbase:vbase + 1024]),
                           "ve")
            last_vve[sbuf_slot] = ve_tok
            buf_free_ve[tb % 2] = ve_tok
            p.wait("sy", "ve", ve_tok)
            v_dma.append(p.inc("sy", lambda e, vst=vst, tb=tb: e.dma_start(
                out=p.kvin.ap()[:, :, 2048 + tb * 128:2048 + tb * 128 + 128]
                    .rearrange("h p c -> p h c"),
                in_=vst.rearrange("p (h c) -> p h c", h=8)), "kb", 16))
            if "v" in p.tap_out and L == p.tap_layer:
                p.inc("sy", lambda e, vst=vst, tb=tb: e.dma_start(
                    out=p.tap_out["v"][tb], in_=vst), "dxt", 16)

        groups2 = [[0, 1], [2, 3], [4, 5], [6, 7]]
        ag_toks = []
        p.wait("gp", "kb", p.n("kb"))
        for hp in range(NHP):
            ag_toks.append(p.inc("gp", lambda e, hp=hp: e.collective_compute(
                "AllGather", ALU.bypass, replica_groups=groups2,
                ins=[p.kvin.ap()[hp]], outs=[p.kvout.ap()[hp]]), "cc"))

        p.arn.reset(0)
        Parn = p.arn.take(4096)
        QT = [p.arn.take(TOK) for _ in range(2)]
        KT = [[p.arn.take(TOK) for _ in range(2)] for _ in range(2)]
        VT = [[[p.arn.take(NBLK * 65) for _ in range(2)] for _ in range(2)]
              for _ in range(2)]
        astage = p.arn.take(1024)

        def emit_readback(hp, buf):
            sn = "rb%d" % (hp % 2)
            p.wait("sy", "cc", ag_toks[hp])
            p.wait("sy", "te", p.n("te"))
            for par in range(2):
                p.inc("sy", lambda e, hp=hp, par=par, buf=buf:
                      e.dma_start(
                    out=KT[buf][par],
                    in_=p.kvout.ap()[hp][128 * par:128 * par + 128, 0:2048]),
                    sn, 16)
                for h2 in range(2):
                    p.inc("sy", lambda e, hp=hp, par=par,
                          h2=h2, buf=buf: e.dma_start(
                        out=VT[buf][h2][par].rearrange(
                            "p (b c) -> p b c", b=NBLK)[:, :, 0:64],
                        in_=p.kvout.ap()[hp][128 * par:128 * par + 128,
                                             2048:4096].rearrange(
                            "p (b hh c) -> p b hh c", b=NBLK, hh=2)
                        [:, :, h2, :]), sn, 16)
            tok = p.inc("sy", lambda e, hp=hp, buf=buf: e.dma_start(
                out=QT[buf], in_=p.qspill.ap()[hp]), sn, 16)
            return (sn, tok)

        def emit_vones(buf):
            vv = 0
            for h2 in range(2):
                for par in range(2):
                    ap = VT[buf][h2][par].rearrange(
                        "p (b c) -> p b c", b=NBLK)[:, :, 64:65]
                    vv = p.inc("ve", lambda e, ap=ap: e.memset(ap, 1.0), "ve")
            return vv

        O_OFF = {0: 2048, 1: 3072}
        rb_toks = {0: emit_readback(0, 0)}
        acts_dma = []
        prev_div_ve = 0
        for hp in range(NHP):
            buf = hp % 2
            if hp + 1 < NHP:
                rb_toks[hp + 1] = emit_readback(hp + 1, 1 - buf)
            p.wait("ve", rb_toks[hp][0], rb_toks[hp][1])
            p.wait("ve", "te", p.n("te"))
            vones_ve = emit_vones(buf)
            for qc in range(2):
                kmax = ext[8 * qc + 7]
                lastk = {}
                for k2 in range(kmax):
                    lo2 = _sub_lo(L, qc, k2)
                    for r in (0, 512):
                        if lo2 < r + 512:
                            lastk[r] = k2
                exp_tok, pv_tok, mask_ve = {}, {}, {}
                for k in range(kmax):
                    lo = _sub_lo(L, qc, k)
                    par, kb = k % 2, k // 2
                    pbuf = k % 2
                    p.wait("te", rb_toks[hp][0], rb_toks[hp][1])
                    p.wait("te", "ve",
                           max(vones_ve, ccol_ve, prev_div_ve))
                    if k >= 1:
                        p.wait("te", "se", exp_tok[k - 1][1])
                    for h2 in range(2):
                        for (a, b_) in _pieces(lo, 1024):
                            p.emit("te", lambda e, h2=h2, a=a, b_=b_,
                                   par=par, kb=kb, qc=qc, buf=buf: e.matmul(
                                p.ps8[:, h2 * 1024 + a:h2 * 1024 + b_],
                                KT[buf][par][64 * h2:64 * h2 + 64,
                                             kb * 128:kb * 128 + 128],
                                QT[buf][64 * h2:64 * h2 + 64,
                                        qc * 1024 + a:qc * 1024 + b_],
                                start=True, stop=True,
                                tile_position=(64 * h2, 0)))
                    s_te = p.te_tok()
                    p.wait("se", "te", s_te)
                    if k >= 2:
                        p.wait("se", "te", pv_tok[k - 2])
                    et = []
                    for h2 in range(2):
                        et.append(p.inc("se", lambda e, h2=h2, lo=lo,
                                        pbuf=pbuf, ccol=ccol: e.activation(
                            Parn[:, pbuf * 2048 + h2 * 1024 + lo:
                                 pbuf * 2048 + h2 * 1024 + 1024],
                            p.ps8[:, h2 * 1024 + lo:h2 * 1024 + 1024],
                            AF.Exp, bias=ccol, scale=0.125), "se"))
                    exp_tok[k] = et
                    need_ve = 0
                    for j in range(8):
                        i = 8 * qc + j
                        if (k, i) in slots and 128 * j >= lo:
                            u = slots[(k, i)]
                            p.wait("ve", "se", et[1])
                            for h2 in range(2):
                                need_ve = p.inc("ve", lambda e, j=j, u=u,
                                                h2=h2, pbuf=pbuf, L=L:
                                                e.tensor_tensor(
                                    Parn[:, pbuf * 2048 + h2 * 1024 + j * 128:
                                         pbuf * 2048 + h2 * 1024 + j * 128
                                         + 128],
                                    Parn[:, pbuf * 2048 + h2 * 1024 + j * 128:
                                         pbuf * 2048 + h2 * 1024 + j * 128
                                         + 128],
                                    p.maskc[L][:, 128 * u:128 * u + 128],
                                    op=ALU.mult), "ve")
                    mask_ve[k] = need_ve
                    p.wait("te", "se", et[1])
                    if need_ve:
                        p.wait("te", "ve", need_ve)
                    for h2 in range(2):
                        for (a, b_) in _pieces(lo, 1024):
                            p.emit("te", lambda e, h2=h2, a=a, b_=b_,
                                   par=par, kb=kb, k=k, buf=buf, pbuf=pbuf,
                                   st=(k == lastk[(a // 512) * 512]): e.matmul(
                                p.ps8[0:65, O_OFF[h2] + a:O_OFF[h2] + b_],
                                VT[buf][h2][par][:, kb * 65:kb * 65 + 65],
                                Parn[:, pbuf * 2048 + h2 * 1024 + a:
                                     pbuf * 2048 + h2 * 1024 + b_],
                                start=(k == 0), stop=st))
                    pv_tok[k] = p.te_tok()
                p.wait("ve", "te", pv_tok[kmax - 1])
                p.emit("ve", lambda e: e.reciprocal(
                    p.rows[64:65, 0:1024], p.ps8[64:65, 2048:3072]))
                dv_ve = p.inc("ve", lambda e: e.reciprocal(
                    p.rows[64:65, 1024:2048], p.ps8[64:65, 3072:4096]), "ve")
                p.wait("te", "ve", dv_ve)
                for h2 in range(2):
                    for half in range(2):
                        p.emit("te", lambda e, h2=h2, half=half: e.matmul(
                            p.ps8[0:64, h2 * 1024 + half * 512:
                                  h2 * 1024 + half * 512 + 512],
                            p.onesF[64:65, 0:64],
                            p.rows[64:65, h2 * 1024 + half * 512:
                                   h2 * 1024 + half * 512 + 512],
                            start=True, stop=True, tile_position=(64, 0)))
                rb_te = p.te_tok()
                p.wait("ve", "te", rb_te)
                dv = 0
                for h2 in range(2):
                    for half in range(2):
                        rbs = p.scr_t[0:64, (2 * h2 + half) * 512:
                                      (2 * h2 + half) * 512 + 512]
                        p.emit("ve", lambda e, h2=h2, half=half, rbs=rbs:
                               e.tensor_copy(
                            rbs, p.ps8[0:64, h2 * 1024 + half * 512:
                                       h2 * 1024 + half * 512 + 512]))
                for h2 in range(2):
                    for half in range(2):
                        rbs = p.scr_t[0:64, (2 * h2 + half) * 512:
                                      (2 * h2 + half) * 512 + 512]
                        dv = p.inc("ve", lambda e, h2=h2, half=half,
                                   rbs=rbs: e.tensor_tensor(
                            astage[64 * h2:64 * h2 + 64,
                                   half * 512:half * 512 + 512],
                            p.ps8[0:64, O_OFF[h2] + half * 512:
                                  O_OFF[h2] + half * 512 + 512],
                            rbs, op=ALU.mult), "ve")
                prev_div_ve = dv
                p.wait("sy", "ve", dv)
                acts_dma.append(p.inc("sy", lambda e, hp=hp, qc=qc:
                                      e.dma_start(
                    out=p.actsb.ap()[hp][:, qc * 1024:qc * 1024 + 1024],
                    in_=astage), "ac", 16))
                if "acts" in p.tap_out and L == p.tap_layer:
                    p.inc("sy", lambda e, hp=hp, qc=qc: e.dma_start(
                        out=p.tap_out["acts"][hp][:,
                                                  qc * 1024:qc * 1024 + 1024],
                        in_=astage), "dxt", 16)
                p.wait("ve", "ac", p.n("ac"))

        p.wb.reset(0)
        p.wait("gp", "te", p.n("te"))
        p.arn.reset(0)
        ast = [p.arn.take(512) for _ in range(16)]
        opan, otoks = [], []
        for oc in range(NCH):
            pan, tk = p.wload(ow[oc], D, "w3")
            opan.append(pan)
            otoks.append(tk)
        mm_toks = {}
        prev_evac = 0
        for ts in range(4):
            p.wait("sy", "ac", acts_dma[-1])
            if ts >= 2:
                p.wait("sy", "te", mm_toks.get(ts - 2, 0))
            ld_tok = 0
            for hp in range(NHP):
                slot = (ts % 2) * 8 + hp
                ld_tok = p.inc("sy", lambda e, ts=ts, hp=hp, slot=slot:
                               e.dma_start(
                    out=ast[slot],
                    in_=p.actsb.ap()[hp][:, ts * 512:ts * 512 + 512]),
                    "la", 16)
            p.wait("te", "la", ld_tok)
            p.wait("te", "w3", otoks[-1])
            p.wait("te", "ve", prev_evac)
            for oc in range(NCH):
                for hp in range(NHP):
                    slot = (ts % 2) * 8 + hp
                    p.emit("te", lambda e, oc=oc, hp=hp, slot=slot: e.matmul(
                        p.ps8[:, oc * 512:oc * 512 + 512],
                        opan[oc][:, hp * 128:hp * 128 + 128],
                        ast[slot], start=(hp == 0), stop=(hp == NHP - 1)))
            mm_toks[ts] = p.te_tok()
            p.wait("ve", "te", mm_toks[ts])
            for oc in range(NCH):
                xsl = p.x_t[:, oc * TOK + ts * 512:oc * TOK + ts * 512 + 512]
                if L == 0:
                    prev_evac = p.inc("ve", lambda e, oc=oc, xsl=xsl:
                                      e.tensor_tensor(
                        xsl, xsl, p.ps8[:, oc * 512:oc * 512 + 512],
                        op=ALU.add), "ve")
                else:
                    prev_evac = p.inc("ve", lambda e, oc=oc, xsl=xsl:
                                      e.scalar_tensor_tensor(
                        xsl, p.ps8[:, oc * 512:oc * 512 + 512],
                        p.lcol[:, 8 + oc:9 + oc], xsl,
                        op0=ALU.add, op1=ALU.add), "ve")
        return prev_evac

    def ffn0(self):
        p = self
        p.arn.reset(0)
        p.scr.reset(0)
        p.wb.reset(0)
        stats_se = p.ln_stats()
        xn_off = 0
        p.arn.take(NCH * TOK)
        vv_xn = p.xn_full(stats_se, xn_off)

        p.wait("gp", "te", p.n("te"))
        wgp, wg_tok = p.wload(p.w["l0_wg"][0], 64, "w2")
        p.wait("te", "w2", wg_tok)
        p.wait("te", "ve", vv_xn)
        for c in range(NCH):
            for (lo, hi) in _pieces(0, TOK):
                p.emit("te", lambda e, c=c, lo=lo, hi=hi, wgp=wgp: e.matmul(
                    p.ps8[0:8, lo:hi], wgp[:, c * 8:c * 8 + 8],
                    p.arn_t[:, xn_off + c * TOK + lo:xn_off + c * TOK + hi],
                    start=(c == 0), stop=(c == NCH - 1)))
        lg_te = p.te_tok()
        lgT = p.rows[0:8, 2048:4096]
        p.wait("ve", "te", lg_te)
        lg_ve = p.inc("ve", lambda e, lgT=lgT: e.tensor_copy(
            lgT, p.ps8[0:8, 0:2048]), "ve")

        p.wait("te", "ve", lg_ve)
        for t in range(NBLK):
            p.emit("te", lambda e, t=t, lgT=lgT: e.transpose(
                p.ps8[:, 2048 + 8 * t:2048 + 8 * t + 8],
                lgT[:, 128 * t:128 * t + 128], p.identF[0:8, 0:8]))
        tr_te = p.te_tok()

        ohall = p.scr_t[:, 0:128]
        Gsc = p.scr_t[:, 128:256]
        cntXm = p.scr_t[:, 256:384]
        cnt128 = p.scr_t[:, 384:392]
        prb = [p.scr_t[:, 392:400], p.scr_t[:, 432:440]]
        tmp8 = [p.scr_t[:, 400:408], p.scr_t[:, 448:456]]
        oh1 = [p.scr_t[:, 440:448], p.scr_t[:, 456:464]]
        nmx = [p.scr_t[:, 464:465], p.scr_t[:, 465:466]]
        rcp = [p.scr_t[:, 466:467], p.scr_t[:, 467:468]]
        e1c = [p.scr_t[:, 468:469], p.scr_t[:, 469:470]]
        rtk = [p.scr_t[:, 470:471], p.scr_t[:, 471:472]]
        gfin = [p.scr_t[:, 472:473], p.scr_t[:, 473:474]]
        p.wait("ve", "te", tr_te)
        p.wait("se", "te", tr_te)
        oh_ve = 0
        for t0 in range(0, NBLK, 2):
            lg2 = [p.ps8[:, 2048 + 8 * t:2048 + 8 * t + 8]
                   for t in (t0, t0 + 1)]
            for s2 in range(2):
                p.emit("ve", lambda e, lg=lg2[s2], s2=s2: e.reduce_max(
                    nmx[s2], lg, axis=AX.X, negate=True))
            p.vnop()
            vv = 0
            for s2 in range(2):
                vv = p.inc("ve", lambda e, lg=lg2[s2], s2=s2:
                           e.tensor_scalar(
                    tmp8[s2], lg, nmx[s2], 0.0, op0=ALU.add,
                    op1=ALU.is_ge), "ve")
            p.wait("se", "ve", vv)
            sv = 0
            for s2 in range(2):
                sv = p.inc("se", lambda e, lg=lg2[s2], s2=s2: e.activation(
                    prb[s2], lg, AF.Exp, bias=nmx[s2]), "se")
            p.wait("ve", "se", sv)
            for s2 in range(2):
                p.emit("ve", lambda e, s2=s2: e.reduce_sum(
                    rcp[s2], prb[s2], axis=AX.X))
            for s2 in range(2):
                p.emit("ve", lambda e, s2=s2: e.tensor_tensor(
                    tmp8[s2], tmp8[s2], p.wvec, op=ALU.mult))
            for s2 in range(2):
                p.emit("ve", lambda e, s2=s2: e.reciprocal(
                    rcp[s2], rcp[s2]))
            for s2 in range(2):
                p.emit("ve", lambda e, s2=s2: e.reduce_max(
                    e1c[s2], tmp8[s2], axis=AX.X))
            p.vnop()
            for s2 in range(2):
                p.emit("ve", lambda e, s2=s2: e.tensor_scalar(
                    e1c[s2], e1c[s2], -1.0, 8.0, op0=ALU.mult, op1=ALU.add))
            p.vnop()
            for s2 in range(2):
                p.emit("ve", lambda e, s2=s2: e.tensor_scalar(
                    oh1[s2], p.evec, e1c[s2], None, op0=ALU.is_equal))
            p.vnop()
            for s2 in range(2):
                p.emit("ve", lambda e, s2=s2: e.scalar_tensor_tensor(
                    tmp8[s2], prb[s2], rcp[s2], oh1[s2], op0=ALU.mult,
                    op1=ALU.mult))
            for s2 in range(2):
                p.emit("ve", lambda e, t=t0 + s2, s2=s2: e.tensor_copy(
                    p.scr_t[:, 8 * t:8 * t + 8], oh1[s2]))
            for s2 in range(2):
                oh_ve = p.inc("ve", lambda e, t=t0 + s2, s2=s2: e.reduce_sum(
                    p.scr_t[:, 416 + t:417 + t], tmp8[s2], axis=AX.X), "ve")

        p.wait("te", "ve", oh_ve)
        p.emit("te", lambda e: e.matmul(
            p.ps8[0:1, 2304:2432], p.onesF[:, 0:1], ohall,
            start=True, stop=True))
        cnt_te = p.te_tok()
        totrow = p.rows[0:1, 0:128]
        p.wait("ve", "te", cnt_te)
        cnt_ve = p.inc("ve", lambda e, totrow=totrow: e.tensor_copy(
            totrow, p.ps8[0:1, 2304:2432]), "ve")
        p.wait("sy", "ve", cnt_ve)
        ci = p.inc("sy", lambda e, totrow=totrow: e.dma_start(
            out=p.cntin.ap(), in_=totrow), "ct", 16)
        p.wait("gp", "ct", ci)
        ag = p.inc("gp", lambda e: e.collective_compute(
            "AllGather", ALU.bypass,
            replica_groups=[list(range(NCORES))],
            ins=[p.cntin.ap()], outs=[p.cntout.ap()]), "cc")
        p.wait("sy", "cc", ag)
        cb = p.inc("sy", lambda e: e.dma_start(
            out=cnt128, in_=p.cntout.ap().rearrange(
                "r (t e) -> (r t) e", e=8)), "ct", 16)
        p.wait("ve", "ct", cb)
        xv = 0
        for t in range(NBLK):
            xv = p.inc("ve", lambda e, t=t: e.tensor_tensor(
                cntXm[:, 8 * t:8 * t + 8], p.maskx[:, 8 * t:8 * t + 8],
                cnt128, op=ALU.mult), "ve")
        p.wait("te", "ve", xv)
        p.emit("te", lambda e: e.matmul(
            p.ps8[0:1, 2432:2560], p.onesF[:, 0:1], cntXm,
            start=True, stop=True))
        off_te = p.te_tok()
        offrow = p.rows[0:1, 128:256]
        p.wait("ve", "te", off_te)
        off_ve = p.inc("ve", lambda e, offrow=offrow: e.tensor_copy(
            offrow, p.ps8[0:1, 2432:2560]), "ve")

        p.wait("te", "ve", off_ve)
        ob = p.inc("te", lambda e, offrow=offrow: e.matmul(
            p.ps8[:, 2560:2688], p.onesF[0:1, 0:128], offrow,
            start=True, stop=False), "te")
        p.wait("te", "te", ob)
        p.emit("te", lambda e: e.matmul(
            p.ps8[:, 2560:2688], p.ltri, ohall, start=False, stop=True))
        rk_te = p.te_tok()

        p.wait("ve", "te", rk_te)
        gv = 0
        for t0 in range(0, NBLK, 2):
            for s2 in range(2):
                p.emit("ve", lambda e, t=t0 + s2, s2=s2:
                       e.scalar_tensor_tensor(
                    tmp8[s2], p.ps8[:, 2560 + 8 * t:2560 + 8 * t + 8], 1.0,
                    p.scr_t[:, 8 * t:8 * t + 8], op0=ALU.mult,
                    op1=ALU.mult))
            p.vnop()
            for s2 in range(2):
                p.emit("ve", lambda e, s2=s2: e.reduce_sum(
                    rtk[s2], tmp8[s2], axis=AX.X))
            p.vnop()
            for s2 in range(2):
                p.emit("ve", lambda e, t=t0 + s2, s2=s2: e.tensor_scalar(
                    gfin[s2], rtk[s2], CAP, p.scr_t[:, 416 + t:417 + t],
                    op0=ALU.is_lt, op1=ALU.mult))
            p.vnop()
            for s2 in range(2):
                gv = p.inc("ve", lambda e, t=t0 + s2, s2=s2: e.tensor_scalar(
                    Gsc[:, 8 * t:8 * t + 8], p.scr_t[:, 8 * t:8 * t + 8],
                    gfin[s2], None, op0=ALU.mult), "ve")
        if "route" in p.tap_out:
            p.wait("sy", "ve", gv)
            p.inc("sy", lambda e: e.dma_start(
                out=p.tap_out["route"], in_=Gsc), "dxt", 16)
        if "scr0" in p.tap_out:
            p.wait("sy", "ve", gv)
            p.inc("sy", lambda e: e.dma_start(
                out=p.tap_out["scr0"], in_=p.scr_t[:, :]), "dxt", 16)

        p.wait("te", "ve", gv)
        p.wait("te", "ve", lg_ve)
        for t in range(NBLK):
            p.emit("te", lambda e, t=t: e.transpose(
                p.ps8[0:8, 128 * t:128 * t + 128],
                Gsc[:, 8 * t:8 * t + 8], p.identF))
        gt_te = p.te_tok()
        gTb = p.arn_t[0:8, NCH * TOK:NCH * TOK + 2048]
        p.wait("ve", "te", gt_te)
        gt_ve = p.inc("ve", lambda e, gTb=gTb: e.tensor_copy(
            gTb, p.ps8[0:8, 0:2048]), "ve")
        p.wait("sy", "ve", gt_ve)
        gb_dma = p.inc("sy", lambda e, gTb=gTb: e.dma_start(
            out=p.gbounce.ap(), in_=gTb), "ct", 16)

        A0 = NCH * TOK + 2048
        bpan = p.arn_t[0:8, A0:A0 + D]
        p.wait("gp", "te", p.n("te"))
        bp_tok = p.inc("gp", lambda e, bpan=bpan: e.dma_start(
            out=bpan, in_=p.w["l0_b"]), "w2", 16)
        geAll = p.arn_t[0:1, A0 + D:A0 + D + TOK]
        ge_tok = 0
        zt = [p.arn_t[:, A0 + D + TOK + i * 512:
                      A0 + D + TOK + (i + 1) * 512] for i in range(8)]
        gBe = p.arn_t[:, A0 + D + TOK + 4096:A0 + D + TOK + 4096 + 512]
        prev_mm = p.n("te")
        prev_evac_ve = 0
        prev_gbe = 0
        import os as _os
        _elist = ([7] if _os.environ.get("MK_ONLY_E7") else list(range(8)))
        for e_ in _elist:
            p.wb.reset(0)
            p.wait("gp", "te", prev_mm)
            wpA, tkA = p.wload(p.w["l0_w"][e_][:, 0:4096], 4096, "w0")
            p.wb.reset(4224)
            wpB, tkB = p.wload(p.w["l0_w"][e_][:, 4096:8192], 4096, "w1")
            p.wait("sy", "ct", gb_dma)
            p.wait("sy", "ve", prev_gbe)
            ge_tok = p.inc("sy", lambda e, e_=e_, geAll=geAll: e.dma_start(
                out=geAll, in_=p.gbounce.ap()[e_:e_ + 1, :]), "ge", 16)
            for tc in range(4):
                p.wait("te", "ge", ge_tok)
                p.wait("te", "ve", prev_evac_ve)
                gb_te = p.inc("te", lambda e, tc=tc, geAll=geAll: e.matmul(
                    p.ps8[:, 3584:4096], p.onesRowB,
                    geAll[:, tc * 512:tc * 512 + 512],
                    start=True, stop=True), "te")
                p.wait("ve", "te", gb_te)
                gb_ve = p.inc("ve", lambda e, gBe=gBe: e.tensor_copy(
                    gBe, p.ps8[:, 3584:4096]), "ve")
                prev_gbe = gb_ve
                zv = 0
                p.wait("ve", "te", prev_mm)
                p.vnop()
                for c in range(NCH):
                    zv = p.inc("ve", lambda e, c=c, tc=tc, gBe=gBe:
                               e.tensor_tensor(
                        zt[c], p.arn_t[:, xn_off + c * TOK + tc * 512:
                                       xn_off + c * TOK + tc * 512 + 512],
                        gBe, op=ALU.mult), "ve")
                p.wait("te", "ve", zv)
                p.wait("te", "w0", tkA)
                p.wait("te", "w1", tkB)
                for f in range(8):
                    wp = wpA if f < 4 else wpB
                    fo = f % 4
                    for c in range(NCH):
                        p.emit("te", lambda e, f=f, fo=fo, c=c, wp=wp,
                               e_=e_: e.matmul(
                            p.ps8[:, f * 512:f * 512 + 512],
                            wp[:, fo * D + c * 128:fo * D + c * 128 + 128],
                            zt[c], start=(c == 0),
                            stop=(c == NCH - 1 and e_ != 7)))
                if e_ == 7:
                    p.wait("te", "w2", bp_tok)
                    p.wait("te", "ve", gt_ve)
                    for f in range(8):
                        p.emit("te", lambda e, f=f, tc=tc, gTb=gTb,
                               bpan=bpan: e.matmul(
                            p.ps8[:, f * 512:f * 512 + 512],
                            bpan[:, f * 128:f * 128 + 128],
                            gTb[:, tc * 512:tc * 512 + 512],
                            start=False, stop=True))
                prev_mm = p.te_tok()
                p.wait("ve", "te", prev_mm)
                for f in range(8):
                    xsl = p.x_t[:, f * TOK + tc * 512:
                                f * TOK + tc * 512 + 512]
                    prev_evac_ve = p.inc("ve", lambda e, f=f, xsl=xsl:
                                         e.tensor_tensor(
                        xsl, xsl, p.ps8[:, f * 512:f * 512 + 512],
                        op=ALU.add), "ve")
        return prev_evac_ve

    def ffn1(self):
        p = self
        p.arn.reset(0)
        p.scr.reset(0)
        p.wb.reset(0)
        stats_se = p.ln_stats()
        xn_off = 0
        p.arn.take(NCH * TOK)
        vv_xn = p.xn_full(stats_se, xn_off)
        h_off = NCH * TOK
        p.arn.take(8 * TOK)

        if "scr0" in p.tap_out:
            p.wait("ve", "dxt", p.n("dxt"))
        p.wait("te", "ve", vv_xn)
        sew_ve = 0
        for (lo, hi) in _pieces(0, TOK):
            p.emit("te", lambda e, lo=lo, hi=hi: e.matmul(
                p.ps8[:, 3584:3584 + hi - lo], p.onesF[32:33, 0:128],
                p.sewrow[:, lo:hi], start=True, stop=True,
                tile_position=(32, 0)))
            sw_te = p.te_tok()
            p.wait("ve", "te", sw_te)
            sew_ve = p.inc("ve", lambda e, lo=lo, hi=hi: e.tensor_copy(
                p.scr_t[:, lo:hi], p.ps8[:, 3584:3584 + hi - lo]), "ve")
            p.wait("te", "ve", sew_ve)

        hsp_dma = {}
        prev_gelu = {}
        prev_wmm = p.n("te")
        for jg in range(16):
            hslot = (jg % 2) * 4 * TOK
            p.wb.reset((jg % 2) * 4224)
            p.wait("gp", "te", prev_wmm)
            w1p, w1t = p.wload(p.w["l1_w1"][jg], 4096, "w%d" % (jg % 2))
            p.wait("te", "w%d" % (jg % 2), w1t)
            jg_se = 0
            for j4 in range(4):
                j = jg * 4 + j4
                pb = 2048 * (j % 2)
                p.wait("te", "se", prev_gelu.get(j - 2, 0))
                for c in range(NCH):
                    for (lo, hi) in _pieces(0, TOK):
                        p.emit("te", lambda e, j4=j4, c=c, lo=lo, hi=hi,
                               pb=pb, w1p=w1p: e.matmul(
                            p.ps8[:, pb + lo:pb + hi],
                            w1p[:, j4 * D + c * 128:j4 * D + c * 128 + 128],
                            p.arn_t[:, xn_off + c * TOK + lo:
                                    xn_off + c * TOK + hi],
                            start=(c == 0), stop=(c == NCH - 1)))
                h_te = p.te_tok()
                prev_wmm = h_te
                p.wait("se", "te", h_te)
                if jg >= 2:
                    p.wait("se", "hs", hsp_dma[jg - 1])
                prev_gelu[j] = p.inc("se", lambda e, j4=j4, pb=pb,
                                     hslot=hslot: e.activation(
                    p.arn_t[:, h_off + hslot + j4 * TOK:
                            h_off + hslot + (j4 + 1) * TOK],
                    p.ps8[:, pb:pb + TOK], AF.Gelu_apprx_tanh), "se")
                jg_se = prev_gelu[j]
            p.wait("sy", "se", jg_se)
            hsp_dma[jg] = p.inc("sy", lambda e, jg=jg, hslot=hslot:
                                e.dma_start(
                out=p.hspill.ap()[jg // 2][:, (jg % 2) * 8192:
                                           (jg % 2) * 8192 + 8192],
                in_=p.arn_t[:, h_off + hslot:h_off + hslot + 4 * TOK]),
                "hs", 16)

        prev_y = 0
        prev_hjg = {}
        round_te = prev_wmm
        for fg in range(2):
            for ts in range(2):
                first = True
                for jg2 in range(8):
                    p.wb.reset((jg2 % 2) * 4224)
                    p.wait("gp", "te", max(prev_hjg.get(jg2 - 2, 0),
                                           round_te))
                    w2p, w2t = p.wload(p.w["l1_w2"][fg][jg2], 4096,
                                       "w%d" % (2 + jg2 % 2))
                    hj = p.arn_t[:, h_off + (jg2 % 2) * 8192:
                                 h_off + (jg2 % 2) * 8192 + 8192]
                    p.wait("sy", "hs", hsp_dma[15])
                    p.wait("sy", "te", max(prev_hjg.get(jg2 - 2, 0),
                                           round_te))
                    hj_tok = p.inc("sy", lambda e, jg2=jg2, ts=ts, hj=hj:
                                   e.dma_start(
                        out=hj.rearrange("p (a b) -> p a b", a=8),
                        in_=p.hspill.ap()[jg2].rearrange(
                            "p (a b) -> p a b", a=8)
                        [:, :, ts * 1024:ts * 1024 + 1024]),
                        "hr%d" % (jg2 % 2), 16)
                    p.wait("te", "w%d" % (2 + jg2 % 2), w2t)
                    p.wait("te", "hr%d" % (jg2 % 2), hj_tok)
                    if first:
                        p.wait("te", "ve", prev_y)
                    for f in range(4):
                        for j in range(8):
                            for half in range(2):
                                p.emit("te", lambda e, f=f, j=j, half=half,
                                       w2p=w2p, hj=hj,
                                       st0=(first and j == 0),
                                       st1=(jg2 == 7 and j == 7): e.matmul(
                                    p.ps8[:, f * 1024 + half * 512:
                                          f * 1024 + half * 512 + 512],
                                    w2p[:, j * 512 + f * 128:
                                        j * 512 + f * 128 + 128],
                                    hj[:, j * 1024 + half * 512:
                                       j * 1024 + half * 512 + 512],
                                    start=st0, stop=st1))
                    first = False
                    prev_hjg[jg2] = p.te_tok()
                y_te = prev_hjg[7]
                p.wait("ve", "te", y_te)
                p.wait("ve", "ve", sew_ve)
                dv = 0
                for f in range(4):
                    p.emit("ve", lambda e, f=f, ts=ts: e.tensor_tensor(
                        p.ps8[:, f * 1024:f * 1024 + 1024],
                        p.ps8[:, f * 1024:f * 1024 + 1024],
                        p.scr_t[:, ts * 1024:ts * 1024 + 1024],
                        op=ALU.mult))
                for f in range(4):
                    fo = fg * 4 + f
                    xsl = p.x_t[:, fo * TOK + ts * 1024:
                                fo * TOK + ts * 1024 + 1024]
                    dv = p.inc("ve", lambda e, f=f, xsl=xsl: e.tensor_tensor(
                        xsl, xsl, p.ps8[:, f * 1024:f * 1024 + 1024],
                        op=ALU.add), "ve")
                prev_y = dv
                round_te = y_te
                prev_hjg = {}
        return prev_y

    def store_out(self, guard_ve):
        p = self
        p.wait("sy", "ve", p.n("ve"))
        for c in range(NCH):
            p.inc("sy", lambda e, c=c: e.dma_start(
                out=p.out_x[c], in_=p.x_t[:, c * TOK:(c + 1) * TOK]),
                "dxo", 16)
        p.wait("sy", "dxo", p.n("dxo"))
        if p.taps:
            p.wait("sy", "dxt", p.n("dxt"))

    def finish(self):
        q = self.q
        with self.nc.Block() as block:
            @block.tensor
            def _(e):
                for f in q["te"]:
                    f(e)

            @block.vector
            def _(e):
                for f in q["ve"]:
                    f(e)

            @block.scalar
            def _(e):
                for f in q["se"]:
                    f(e)

            @block.gpsimd
            def _(e):
                for f in q["gp"]:
                    f(e)

            @block.sync
            def _(e):
                for f in q["sy"]:
                    f(e)
        return self.nc



def build_nc(taps=(), n_layers=N_LAYERS, tap_layer=0, dump=()):
    mk = MK(taps=taps, n_layers=n_layers, tap_layer=tap_layer, dump=dump)
    mk.init_loads()

    def xtap(name):
        if name in mk.tap_out:
            mk.wait("sy", "ve", mk.n("ve"))
            for c in range(NCH):
                mk.inc("sy", lambda e, c=c, name=name: e.dma_start(
                    out=mk.tap_out[name][c],
                    in_=mk.x_t[:, c * TOK:(c + 1) * TOK]), "dxt", 16)

    g = 0
    for L in range(n_layers):
        g = mk.attention(L)
        xtap("x1" if L == 0 else "x3")
        g = mk.ffn0() if L == 0 else mk.ffn1()
        if L == 0:
            xtap("x2")
    mk.store_out(g)
    return mk.finish()


def _kq_panels(wT_perm):
    t = wT_perm.reshape(NCH, 128, NCH, 128)
    return np.ascontiguousarray(
        np.transpose(t, (2, 1, 0, 3)).reshape(NCH, 128, D), dtype=np.float32)


def _row_dim_map():
    r = np.arange(128)
    half = r % 64
    return np.where(half < 32, 2 * (half % 32), 2 * (half - 32) + 1)


def _host_inputs(inputs, core):
    b, par = core // 2, core % 2
    pos = _own_positions(par)
    out = {}
    X = np.asarray(inputs["L0_attn_X"])
    out["x"] = np.ascontiguousarray(
        X[b].T[:, pos].reshape(NCH, 128, TOK), dtype=np.float32)

    inv = 1.0 / (10000.0 ** (np.arange(0, DH, 2, dtype=np.float64) / DH))
    ang = inv[:, None] * pos[None, :].astype(np.float64)
    cs = np.cos(ang)
    sn = np.sin(ang)
    out["csB"] = np.ascontiguousarray(np.tile(cs, (4, 1)), dtype=np.float32)
    out["snB"] = np.ascontiguousarray(
        np.concatenate([sn, -sn, sn, -sn], axis=0), dtype=np.float32)

    cf = np.zeros((128, 532), dtype=np.float32)
    cf[:, 0:128] = 1.0
    tri = np.tril(np.ones((128, 128), np.float32), -1).T
    cf[:, 128:256] = tri
    cf[:, 256:384] = np.eye(128, dtype=np.float32)
    cf[:, 384:392] = np.arange(8, 0, -1, dtype=np.float32)[None, :]
    cf[:, 392:400] = np.arange(8, dtype=np.float32)[None, :]
    cf[0, 400:464] = 1.0
    cf[1, 464:528] = 1.0
    cf[:, 528] = EPS_LN
    cf[:, 529] = EPS_RMS
    out["constf"] = cf

    h16 = np.zeros((128, 768), dtype=np.float32)
    h16[:, 0:128] = 1.0
    h16[0:64, 128] = 1.0
    h16[64:128, 129] = 1.0
    out["h16"] = h16

    for L in range(N_LAYERS):
        out["maskc%d" % L] = _mask_consts(L, par)

    mx = np.zeros((128, 128), dtype=np.float32)
    for r in range(8):
        bp, pp = r // 2, r % 2
        for t2 in range(NBLK):
            gother = 2 * t2 + pp
            for t in range(NBLK):
                gmine = 2 * t + par
                if bp < b or (bp == b and gother < gmine):
                    mx[16 * r + t2, 8 * t:8 * t + 8] = 1.0
    out["maskx"] = mx

    lc = np.zeros((128, 16), dtype=np.float32)
    dimmap = _row_dim_map()
    qn = np.asarray(inputs["L1_attn_qn_w"])
    kn = np.asarray(inputs["L1_attn_kn_w"])
    lc[:, 0] = qn[dimmap]
    lc[:, 1] = qn[dimmap[np.arange(128) ^ 32]]
    lc[:, 2] = kn[dimmap]
    lc[:, 3] = kn[dimmap[np.arange(128) ^ 32]]
    bproj = np.asarray(inputs["L1_attn_bproj"])
    for oc in range(NCH):
        lc[:, 8 + oc] = bproj[oc * 128:(oc + 1) * 128]
    out["lcol"] = lc

    sew = np.zeros((1, 2 * TOK), dtype=np.float32)
    if b == 0:
        ew = np.asarray(inputs["L1_ffn_expert_weights"])
        sew[0, 0:TOK] = ew[pos, 0]
        sew[0, TOK:] = ew[pos, 1]
    out["sew"] = sew

    perm = _perm_kq()
    wkqv = np.asarray(inputs["L0_attn_W_KQV"]).T
    out["l0_kq"] = np.concatenate(
        [_kq_panels(wkqv[:, 0:D][:, perm]),
         _kq_panels(wkqv[:, D:2 * D][:, perm])], axis=0)
    out["l0_v"] = np.ascontiguousarray(
        wkqv[:, 2 * D:3 * D].reshape(NCH, 128, D), dtype=np.float32)
    out["l0_o"] = _kq_panels(np.asarray(inputs["L0_attn_W_O"]).T)
    wg = np.asarray(inputs["L0_ffn_wg"])
    out["l0_wg"] = np.ascontiguousarray(
        wg.reshape(1, NCH, 128, 8).transpose(0, 2, 1, 3).reshape(1, 128, 64),
        dtype=np.float32)
    out["l0_b"] = np.ascontiguousarray(inputs["L0_ffn_b"], dtype=np.float32)
    W0 = np.asarray(inputs["L0_ffn_W"])
    t = W0.reshape(8, NCH, 128, NCH, 128)
    out["l0_w"] = np.ascontiguousarray(
        np.transpose(t, (0, 4, 1, 3, 2)).reshape(8, 128, 8192),
        dtype=np.float32)

    wqkv = np.asarray(inputs["L1_attn_Wqkv"])
    out["l1_kq"] = np.concatenate(
        [_kq_panels(wqkv[:, D:2 * D][:, perm]),
         _kq_panels(wqkv[:, 0:D][:, perm])], axis=0)
    out["l1_v"] = np.ascontiguousarray(
        wqkv[:, 2 * D:3 * D].reshape(NCH, 128, D), dtype=np.float32)
    out["l1_o"] = _kq_panels(np.asarray(inputs["L1_attn_Wproj"]))
    w1 = np.asarray(inputs["L1_ffn_w1"])
    t = w1.reshape(NCH, 128, 64, 128)
    out["l1_w1"] = np.ascontiguousarray(
        np.transpose(t, (2, 1, 0, 3)).reshape(64, 128, D)
        .reshape(16, 4, 128, D)[:, :, :, :]
        .transpose(0, 2, 1, 3).reshape(16, 128, 4096), dtype=np.float32)
    w2 = np.asarray(inputs["L1_ffn_w2"])
    t = w2.reshape(8, 8, 128, 2, 4, 128)
    out["l1_w2"] = np.ascontiguousarray(
        np.transpose(t, (3, 0, 2, 1, 4, 5)).reshape(2, 8, 128, 4096),
        dtype=np.float32)
    return out


_NC_CACHE = {}


def kernel(**inputs):
    key = "full"
    if key not in _NC_CACHE:
        _NC_CACHE[key] = build_nc()
    nc = _NC_CACHE[key]
    in_maps = [_host_inputs(inputs, c) for c in range(NCORES)]
    res = run_bass_kernel_spmd(nc, in_maps, core_ids=list(range(NCORES)))
    X = np.asarray(inputs["L0_attn_X"])
    out = np.empty_like(X)
    for c in range(NCORES):
        b, par = c // 2, c % 2
        pos = _own_positions(par)
        xf = res.results[c]["out_x"].reshape(D, TOK)
        out[b, pos, :] = xf.T
    return out
